# version 04
# baseline (speedup 1.0000x reference)
import math
import numpy as np
import concourse.bass as bass
import concourse.mybir as mybir
from concourse.bass_utils import run_bass_kernel_spmd

F32 = mybir.dt.float32
BF16 = mybir.dt.bfloat16
AF = mybir.ActivationFunctionType
ALU = mybir.AluOpType
AX = mybir.AxisListType

B, T, C = 2, 4096, 2048
NQ, NKV, HD = 16, 4, 128
INTER = 5632
NCC = C // 128
NIC = INTER // 128
TOWN = 1024
NST = T // 512
EPS = 1e-5
ASCALE = 1.0 / math.sqrt(HD)
ROPE_BASE = 10000.0
NEG = -1.0e30
ARENA_COLS = 34048
BARENA_COLS = 28672

ENGS = ("pe", "act", "dve", "sp", "gp")


class Slot:
    __slots__ = ("w", "rs")

    def __init__(self):
        self.w = None
        self.rs = []

    def rdeps(self):
        return [self.w] if self.w else []

    def wdeps(self):
        d = list(self.rs)
        if self.w:
            d.append(self.w)
        return d


class Sched:
    NQSEM = 8
    SELF_WINDOW = 8
    SELF_SYNC_ENGS = ("act", "dve")

    def __init__(self):
        self.sem_names = ["pe", "act", "dve"]
        for q in ("sp", "gp"):
            self.sem_names += ["%s%d" % (q, i) for i in range(self.NQSEM)]
        self.ops = {e: [] for e in ENGS}
        self.cnt = {s: 0 for s in self.sem_names}
        self.ecnt = {e: 0 for e in ENGS}
        self.dma_i = {"sp": 0, "gp": 0}
        self.pending_barrier = {}

    def barrier(self):
        snap = {s: v for s, v in self.cnt.items() if v > 0}
        for e in ENGS:
            old = self.pending_barrier.get(e, {})
            for s, v in snap.items():
                if old.get(s, 0) < v:
                    old[s] = v
            self.pending_barrier[e] = old

    def op(self, eng, fn, reads=(), writes=(), inc=None):
        deps = []
        for s in reads:
            deps += s.rdeps()
        for s in writes:
            deps += s.wdeps()
        need = self.pending_barrier.pop(eng, {})
        own = eng if eng in ("pe", "act", "dve") else None
        for (sem, v) in deps:
            if sem == own:
                if (eng in self.SELF_SYNC_ENGS and v > self.cnt[eng] - self.SELF_WINDOW
                        and need.get(sem, 0) < v):
                    need[sem] = v
            elif eng == "pe" and sem == "pe":
                pass
            else:
                if need.get(sem, 0) < v:
                    need[sem] = v
        if eng in ("sp", "gp"):
            sem = "%s%d" % (eng, self.dma_i[eng] % self.NQSEM)
            self.dma_i[eng] += 1
            step = 16
        else:
            sem = eng
            step = 1 if inc is None else inc
        self.cnt[sem] += step
        tok = (sem, self.cnt[sem])
        self.ops[eng].append((fn, need, sem, step))
        self.ecnt[eng] += 1
        for s in reads:
            s.rs.append(tok)
        for s in writes:
            s.w = tok
            s.rs = []
        return tok

    def emit(self, nc):
        sems = {s: nc.alloc_semaphore("sem_" + s) for s in self.sem_names}
        final_gp = [( "gp%d" % i, self.cnt["gp%d" % i]) for i in range(self.NQSEM)]
        with nc.Block() as block:
            @block.tensor
            def _(engine):
                self._run(engine, "pe", sems)
            @block.scalar
            def _(engine):
                self._run(engine, "act", sems)
            @block.vector
            def _(engine):
                self._run(engine, "dve", sems)
            @block.sync
            def _(engine):
                self._run(engine, "sp", sems)
            @block.gpsimd
            def _(engine):
                self._run(engine, "gp", sems)
                for s, v in final_gp:
                    if v > 0:
                        engine.wait_ge(sems[s], v)

    def _run(self, engine, engname, sems):
        water = {s: 0 for s in self.sem_names}
        for (fn, need, sem, step) in self.ops[engname]:
            for s, v in sorted(need.items()):
                if water[s] < v:
                    engine.wait_ge(sems[s], v)
                    water[s] = v
            ins = fn(engine)
            ins.then_inc(sems[sem], step)


def build_program(debug=False):
    nc = bass.Bass("TRN2", target_bir_lowering=False)
    S = Sched()

    def din(name, shape):
        return nc.dram_tensor(name, shape, F32, kind="ExternalInput").ap()

    xall = din("xall", [T, C])
    xown = din("xown", [TOWN, C])
    wq = din("wq", [NQ * 128, C])
    wk = din("wk", [NKV * 128, C])
    wv = din("wv", [C, NKV * 128])
    wp = din("wp", [C, C])
    wg = din("wg", [NIC * 128, C])
    wu = din("wu", [NIC * 128, C])
    wd = din("wd", [INTER, C])
    gs = din("gs", [128, NIC])
    us = din("us", [128, NIC])
    dsb = din("dsb", [128, C])
    tabkc = din("tabkc", [128, T])
    tabks = din("tabks", [128, T])
    tabqc = din("tabqc", [128, TOWN])
    tabqs = din("tabqs", [128, TOWN])
    rotm = din("rotm", [128, 128])
    idm = din("idm", [128, 128])
    onesr = din("onesr", [1, 128])
    maskadd = din("maskadd", [2 * 128, TOWN])

    out = nc.dram_tensor("out", [TOWN, C], F32, kind="ExternalOutput").ap()
    def dscratch(name, shape):
        if debug:
            return nc.dram_tensor(name, shape, F32, kind="ExternalOutput").ap()
        return nc.dram_tensor(name, shape, F32).ap()
    ks = dscratch("ks", [NKV * 128, T])
    vs = dscratch("vs", [T, NKV * 128])
    qs = dscratch("qs", [NQ * 128, TOWN])
    aos = dscratch("aos", [NQ * 128, TOWN])
    x1s = dscratch("x1s", [TOWN, C])

    arena = nc.alloc_sbuf_tensor("arena", [128, ARENA_COLS], F32)
    arenab = nc.alloc_sbuf_tensor("arenab", [128, BARENA_COLS], BF16)
    p_rotm = nc.alloc_sbuf_tensor("p_rotm", [128, 128], F32)
    p_idm = nc.alloc_sbuf_tensor("p_idm", [128, 128], F32)
    p_mask = nc.alloc_sbuf_tensor("p_mask", [128, 2 * TOWN], F32)
    p_gs = nc.alloc_sbuf_tensor("p_gs", [128, NIC], F32)
    p_us = nc.alloc_sbuf_tensor("p_us", [128, NIC], F32)
    p_ds = nc.alloc_sbuf_tensor("p_ds", [128, C], F32)
    p_ones = nc.alloc_sbuf_tensor("p_ones", [1, 128], F32)

    pb = [nc.alloc_psum_tensor("pb%d" % i, [128, 512], F32) for i in range(8)]
    pslot = [Slot() for _ in range(8)]

    class Region:
        def __init__(self):
            self.off = 0
            self.boff = 0
            self.views = {}
            self.slots = {}

        def get(self, name, cols, dt=F32):
            if name not in self.views:
                if dt is F32:
                    assert self.off + cols <= ARENA_COLS, (name, self.off, cols)
                    self.views[name] = arena[:, self.off:self.off + cols]
                    self.off += cols
                else:
                    assert self.boff + cols <= BARENA_COLS, (name, self.boff, cols)
                    self.views[name] = arenab[:, self.boff:self.boff + cols]
                    self.boff += cols
                self.slots[name] = Slot()
            return self.views[name], self.slots[name]

    cslot = {n: Slot() for n in ("rotm", "idm", "mask", "gs", "us", "ds", "ones")}

    def dma_in(dst_ap, src_ap, writes, reads=()):
        return S.op("sp", lambda e, o=dst_ap, i=src_ap: e.dma_start(out=o, in_=i),
                    reads=reads, writes=writes)

    def dma_in_gp(dst_ap, src_ap, writes, reads=()):
        return S.op("gp", lambda e, o=dst_ap, i=src_ap: e.dma_start(out=o, in_=i),
                    reads=reads, writes=writes)

    def dma_out(dst_ap, src_ap, reads):
        return S.op("gp", lambda e, o=dst_ap, i=src_ap: e.dma_start(out=o, in_=i),
                    reads=reads)

    def mm2(bank, out_ap, lhsT, rhs, start, stop, reads):
        return S.op("pe",
                    lambda e, o=out_ap, l=lhsT, rr=rhs, s1=start, s2=stop:
                        e.matmul(o, l, rr, start=s1, stop=s2),
                    reads=list(reads), writes=[pslot[bank]])

    def transpose(bank, out_ap, in_ap, reads):
        return S.op("pe",
                    lambda e, o=out_ap, i=in_ap: e.transpose(o, i, p_idm[:, :]),
                    reads=list(reads) + [cslot["idm"]], writes=[pslot[bank]])

    def act(fn, reads, writes):
        return S.op("act", fn, reads=reads, writes=writes)

    def dve(fn, reads, writes):
        return S.op("dve", fn, reads=reads, writes=writes)

    def ln_tile(R, xv, xslot, xnv, xnslot):
        sq, sq_s = R.get("ln_sq", C)
        st, st_s = R.get("ln_st", 16)
        act(lambda e: e.activation(sq[:, :], xv[:, :], AF.Square,
                                   accum_out=st[:, 2:3]),
            reads=[xslot], writes=[sq_s, st_s])
        dve(lambda e: e.reduce_sum(st[:, 0:1], xv[:, :], axis=AX.X),
            reads=[xslot], writes=[st_s])
        dve(lambda e: e.tensor_scalar(st[:, 1:2], st[:, 0:1], 1.0 / C, None, ALU.mult),
            reads=[st_s], writes=[st_s])
        dve(lambda e: e.tensor_scalar(st[:, 3:4], st[:, 2:3], 1.0 / C, None, ALU.mult),
            reads=[st_s], writes=[st_s])
        dve(lambda e: e.tensor_tensor(st[:, 4:5], st[:, 1:2], st[:, 1:2], ALU.mult),
            reads=[st_s], writes=[st_s])
        dve(lambda e: e.tensor_tensor(st[:, 5:6], st[:, 3:4], st[:, 4:5], ALU.subtract),
            reads=[st_s], writes=[st_s])
        dve(lambda e: e.tensor_scalar(st[:, 10:11], st[:, 5:6], EPS, None, ALU.add),
            reads=[st_s], writes=[st_s])
        act(lambda e: e.activation(st[:, 6:7], st[:, 10:11], AF.Sqrt),
            reads=[st_s], writes=[st_s])
        dve(lambda e: e.reciprocal(st[:, 7:8], st[:, 6:7]),
            reads=[st_s], writes=[st_s])
        dve(lambda e: e.tensor_tensor(st[:, 8:9], st[:, 1:2], st[:, 7:8], ALU.mult),
            reads=[st_s], writes=[st_s])
        dve(lambda e: e.tensor_scalar(st[:, 9:10], st[:, 8:9], -1.0, None, ALU.mult),
            reads=[st_s], writes=[st_s])
        dve(lambda e: e.tensor_scalar(xnv[:, :], xv[:, :], st[:, 7:8], st[:, 9:10],
                                      ALU.mult, ALU.add),
            reads=[xslot, st_s], writes=[xnslot])

    def xpose_tile(R, xnv, xnslot, xntv, xntslot, j, tw, b0=0):
        for grp in range(4):
            for q in range(4):
                cc = grp * 4 + q
                transpose(b0 + grp, pb[b0 + grp][:, q * 128:(q + 1) * 128],
                          xnv[:, cc * 128:(cc + 1) * 128], reads=[xnslot])
            outap = xntv.rearrange("p (c t) -> p c t", t=tw)[:, grp * 4:grp * 4 + 4,
                                                            j * 128:(j + 1) * 128]
            inap = pb[b0 + grp][:, 0:512].rearrange("p (c t) -> p c t", t=128)
            dve(lambda e, o=outap, i=inap: e.tensor_copy(o, i),
                reads=[pslot[b0 + grp]], writes=[xntslot])

    dma_in(p_rotm[:, :], rotm, [cslot["rotm"]])
    dma_in(p_idm[:, :], idm, [cslot["idm"]])
    dma_in(p_mask[:, 0:TOWN], maskadd[0:128, :], [cslot["mask"]])
    dma_in(p_mask[:, TOWN:2 * TOWN], maskadd[128:256, :], [cslot["mask"]])
    dma_in(p_gs[:, :], gs, [cslot["gs"]])
    dma_in(p_us[:, :], us, [cslot["us"]])
    dma_in(p_ds[:, :], dsb, [cslot["ds"]])
    dma_in(p_ones[:, :], onesr, [cslot["ones"]])

    R1 = Region()
    tkc_v, tks_v, tkc_sl, tks_sl = [], [], [], []
    for par in range(2):
        v, s = R1.get("tabkc%d" % par, 512)
        tkc_v.append(v); tkc_sl.append(s)
        v, s = R1.get("tabks%d" % par, 512)
        tks_v.append(v); tks_sl.append(s)
    wkv_v, wkv_s = R1.get("wk", NKV * C)
    for h in range(NKV):
        dma_in(wkv_v[:, h * C:(h + 1) * C], wk[h * 128:(h + 1) * 128, :], [wkv_s])
    xb_v, xb_s = [], []
    for par in range(2):
        v, s = R1.get("xb%d" % par, C)
        xb_v.append(v); xb_s.append(s)
    xn_v, xn_s = R1.get("xn", C)
    xnt_v, xnt_s = R1.get("xnt", 16 * 512)
    wvb_v, wvb_s = [], []
    for par in range(2):
        v, s = R1.get("wvb%d" % par, 512)
        wvb_v.append(v); wvb_s.append(s)
    kraw_v, kraw_s = R1.get("kraw", 512)
    kst_v, kst_s = R1.get("kst", 512)
    ktm_v, ktm_s = R1.get("ktm", 512)
    kfin_v, kfin_s = R1.get("kfin", 512)
    vst_v, vst_s = R1.get("vst", 512)

    for st in range(NST):
        t0 = st * 512
        tp = st % 2
        tkc, tkc_s = tkc_v[tp], tkc_sl[tp]
        tks, tks_s = tks_v[tp], tks_sl[tp]
        dma_in(tkc[:, :], tabkc[:, t0:t0 + 512], [tkc_s])
        dma_in(tks[:, :], tabks[:, t0:t0 + 512], [tks_s])
        for j in range(4):
            p = j % 2
            dma_in(xb_v[p][:, :], xall[t0 + j * 128:t0 + (j + 1) * 128, :], [xb_s[p]])
            ln_tile(R1, xb_v[p], xb_s[p], xn_v, xn_s)
            xpose_tile(R1, xn_v, xn_s, xnt_v, xnt_s, j, 512, b0=0)
        for cc in range(NCC):
            p = cc % 2
            dma_in(wvb_v[p][:, :], wv[cc * 128:(cc + 1) * 128, :], [wvb_s[p]])
            for j in range(4):
                mm2(j, pb[j][:, 0:512], xnt_v[:, cc * 512 + j * 128:cc * 512 + (j + 1) * 128],
                    wvb_v[p][:, :], cc == 0, cc == NCC - 1, [xnt_s, wvb_s[p]])
        for j in range(4):
            act(lambda e, b=j: e.copy(vst_v[:, :], pb[b][:, 0:512]),
                reads=[pslot[j]], writes=[vst_s])
            dma_out(vs[t0 + j * 128:t0 + (j + 1) * 128, :], vst_v[:, :], [vst_s])
        for h in range(NKV):
            bk = 4 + (h % 2)
            br = 6 + (h % 2)
            for cc in range(NCC):
                mm2(bk, pb[bk][:, 0:512], wkv_v[:, h * C + cc * 128:h * C + (cc + 1) * 128],
                    xnt_v[:, cc * 512:(cc + 1) * 512], cc == 0, cc == NCC - 1,
                    [wkv_s, xnt_s])
            act(lambda e, b=bk: e.copy(kraw_v[:, :], pb[b][:, 0:512]),
                reads=[pslot[bk]], writes=[kraw_s])
            mm2(br, pb[br][:, 0:512], p_rotm[:, :], kraw_v[:, :], True, True,
                [cslot["rotm"], kraw_s])
            dve(lambda e, tv=tkc: e.tensor_tensor(kst_v[:, :], kraw_v[:, :],
                                                  tv[:, :], ALU.mult),
                reads=[kraw_s, tkc_s], writes=[kst_s])
            dve(lambda e, b=br, tv=tks: e.tensor_tensor(ktm_v[:, :], pb[b][:, 0:512],
                                                        tv[:, :], ALU.mult),
                reads=[pslot[br], tks_s], writes=[ktm_s])
            dve(lambda e: e.tensor_tensor(kfin_v[:, :], kst_v[:, :], ktm_v[:, :], ALU.add),
                reads=[kst_s, ktm_s], writes=[kfin_s])
            dma_out(ks[h * 128:(h + 1) * 128, t0:t0 + 512], kfin_v[:, :], [kfin_s])

    S.barrier()

    R2 = Region()
    tqc, tqc_s = R2.get("tabqc", TOWN)
    tqs, tqs_s = R2.get("tabqs", TOWN)
    dma_in(tqc[:, :], tabqc, [tqc_s])
    dma_in(tqs[:, :], tabqs, [tqs_s])
    xb2_v, xb2_s = [], []
    for par in range(2):
        v, s = R2.get("xb%d" % par, C)
        xb2_v.append(v); xb2_s.append(s)
    xn2_v, xn2_s = R2.get("xn", C)
    xnt2_v, xnt2_s = [], []
    for qt in range(2):
        v, s = R2.get("xnt%d" % qt, 16 * 512)
        xnt2_v.append(v); xnt2_s.append(s)
    wqb_v, wqb_s = [], []
    for par in range(2):
        v, s = R2.get("wqb%d" % par, C)
        wqb_v.append(v); wqb_s.append(s)
    qraw_v, qraw_s = R2.get("qraw", 512)
    qst_v, qst_s = R2.get("qst", 512)
    qtm_v, qtm_s = R2.get("qtm", 512)
    qfin_v, qfin_s = R2.get("qfin", 512)

    for qt in range(2):
        for j in range(4):
            p = j % 2
            dma_in(xb2_v[p][:, :], xown[qt * 512 + j * 128:qt * 512 + (j + 1) * 128, :],
                   [xb2_s[p]])
            ln_tile(R2, xb2_v[p], xb2_s[p], xn2_v, xn2_s)
            xpose_tile(R2, xn2_v, xn2_s, xnt2_v[qt], xnt2_s[qt], j, 512, b0=0)
    for oc in range(NQ):
        p = oc % 2
        dma_in(wqb_v[p][:, :], wq[oc * 128:(oc + 1) * 128, :], [wqb_s[p]])
        for qt in range(2):
            bk = 4 + (qt % 2)
            br = 6 + (qt % 2)
            t0 = qt * 512
            for cc in range(NCC):
                mm2(bk, pb[bk][:, 0:512], wqb_v[p][:, cc * 128:(cc + 1) * 128],
                    xnt2_v[qt][:, cc * 512:(cc + 1) * 512], cc == 0, cc == NCC - 1,
                    [wqb_s[p], xnt2_s[qt]])
            act(lambda e, b=bk: e.copy(qraw_v[:, :], pb[b][:, 0:512]),
                reads=[pslot[bk]], writes=[qraw_s])
            mm2(br, pb[br][:, 0:512], p_rotm[:, :], qraw_v[:, :], True, True,
                [cslot["rotm"], qraw_s])
            dve(lambda e, t_=t0: e.tensor_tensor(qst_v[:, :], qraw_v[:, :],
                                                 tqc[:, t_:t_ + 512], ALU.mult),
                reads=[qraw_s, tqc_s], writes=[qst_s])
            dve(lambda e, b=br, t_=t0: e.tensor_tensor(qtm_v[:, :], pb[b][:, 0:512],
                                                       tqs[:, t_:t_ + 512], ALU.mult),
                reads=[pslot[br], tqs_s], writes=[qtm_s])
            dve(lambda e: e.tensor_tensor(qfin_v[:, :], qst_v[:, :], qtm_v[:, :], ALU.add),
                reads=[qst_s, qtm_s], writes=[qfin_s])
            dma_out(qs[oc * 128:(oc + 1) * 128, t0:t0 + 512], qfin_v[:, :], [qfin_s])

    S.barrier()

    R3 = Region()
    qtb_v, qtb_s = [], []
    for par in range(2):
        v, s = R3.get("qtb%d" % par, 256)
        qtb_v.append(v); qtb_s.append(s)
    kb_v, kb_s = R3.get("kb", T)
    vb_v, vb_s = R3.get("vb", T)
    srow_v, srow_s, prow_v, prow_s = [], [], [], []
    for h2 in range(2):
        v, s = R3.get("srow%d" % h2, T)
        srow_v.append(v); srow_s.append(s)
        v, s = R3.get("prow%d" % h2, T)
        prow_v.append(v); prow_s.append(s)
    strip_v, strip_s = R3.get("strip", 32)
    sst_v, sst_s = R3.get("sst", 16)
    ptb_v, ptb_s = [], []
    for par in range(2):
        v, s = R3.get("ptb%d" % par, 512)
        ptb_v.append(v); ptb_s.append(s)
    aot_v, aot_s = [], []
    rr_v, rr_s, bc_v, bc_s = [], [], [], []
    for par in range(2):
        v, s = R3.get("aot%d" % par, 256)
        aot_v.append(v); aot_s.append(s)
        v, s = R3.get("rrow%d" % par, 256)
        rr_v.append(v); rr_s.append(s)
        v, s = R3.get("bc%d" % par, 256)
        bc_v.append(v); bc_s.append(s)

    qpar = 0
    for k in range(NKV):
        for g in range(4):
            kvlen = (g + 1) * 1024
            nkv = kvlen // 512
            dma_in(kb_v[:, g * 1024:(g + 1) * 1024],
                   ks[k * 128:(k + 1) * 128, g * 1024:(g + 1) * 1024], [kb_s])
            dma_in(vb_v[:, g * 1024:(g + 1) * 1024].rearrange("p (t d) -> p t d", d=128),
                   vs[g * 1024:(g + 1) * 1024, k * 128:(k + 1) * 128]
                   .rearrange("(t p) d -> p t d", p=128), [vb_s])
            for qi in range(4):
                h16 = k * 4 + qi
                qp = qpar % 2
                qpar += 1
                dma_in(qtb_v[qp][:, :],
                       qs[h16 * 128:(h16 + 1) * 128, g * 256:(g + 1) * 256],
                       [qtb_s[qp]])
                for h2 in range(2):
                    qtile = qtb_v[qp][:, h2 * 128:(h2 + 1) * 128]
                    sr, sr_s = srow_v[h2], srow_s[h2]
                    pr, pr_s = prow_v[h2], prow_s[h2]
                    for t in range(nkv):
                        bk = 4 * h2 + t % 2
                        mm2(bk, pb[bk][:, 0:512], qtile, kb_v[:, t * 512:(t + 1) * 512],
                            True, True, [qtb_s[qp], kb_s])
                        if t >= nkv - 2:
                            moff = h2 * TOWN + (t - (nkv - 2)) * 512
                            dve(lambda e, b=bk, tt=t, mo=moff, o=sr:
                                e.tensor_tensor(o[:, tt * 512:(tt + 1) * 512],
                                                pb[b][:, 0:512],
                                                p_mask[:, mo:mo + 512], ALU.add),
                                reads=[pslot[bk], cslot["mask"]], writes=[sr_s])
                        else:
                            act(lambda e, b=bk, tt=t, o=sr:
                                e.copy(o[:, tt * 512:(tt + 1) * 512], pb[b][:, 0:512]),
                                reads=[pslot[bk]], writes=[sr_s])
                        dve(lambda e, tt=t, i=sr, c=h2 * 8 + t:
                            e.reduce_max(strip_v[:, c:c + 1],
                                         i[:, tt * 512:(tt + 1) * 512], axis=AX.X),
                            reads=[sr_s], writes=[strip_s])
                    c0 = h2 * 4
                    dve(lambda e, n=nkv, c=h2 * 8, c0_=c0:
                        e.reduce_max(sst_v[:, c0_:c0_ + 1], strip_v[:, c:c + n], axis=AX.X),
                        reads=[strip_s], writes=[sst_s])
                    dve(lambda e, c0_=c0: e.tensor_scalar(sst_v[:, c0_ + 1:c0_ + 2],
                                                          sst_v[:, c0_:c0_ + 1],
                                                          -ASCALE, None, ALU.mult),
                        reads=[sst_s], writes=[sst_s])
                    act(lambda e, n=nkv, i=sr, o=pr, c0_=c0:
                        e.activation(o[:, 0:n * 512], i[:, 0:n * 512],
                                     AF.Exp, bias=sst_v[:, c0_ + 1:c0_ + 2], scale=ASCALE,
                                     accum_out=sst_v[:, c0_ + 2:c0_ + 3]),
                        reads=[sr_s, sst_s], writes=[pr_s, sst_s])
                    dve(lambda e, c0_=c0: e.reciprocal(sst_v[:, c0_ + 3:c0_ + 4],
                                                       sst_v[:, c0_ + 2:c0_ + 3]),
                        reads=[sst_s], writes=[sst_s])
                bp = qpar % 2
                for h2 in range(2):
                    transpose(7, pb[7][0:1, h2 * 128:(h2 + 1) * 128],
                              sst_v[:, h2 * 4 + 3:h2 * 4 + 4], [sst_s])
                dve(lambda e, p_=bp: e.tensor_copy(rr_v[p_][0:1, :], pb[7][0:1, 0:256]),
                    reads=[pslot[7]], writes=[rr_s[bp]])
                mm2(7, pb[7][:, 256:512], p_ones[0:1, :], rr_v[bp][0:1, :],
                    True, True, [cslot["ones"], rr_s[bp]])
                dve(lambda e, p_=bp: e.tensor_copy(bc_v[p_][:, :], pb[7][:, 256:512]),
                    reads=[pslot[7]], writes=[bc_s[bp]])
                for gi in range(nkv * 2):
                    pp = gi % 2
                    ptbank = 2 if pp == 0 else 6
                    for sub in range(2):
                        tb = gi * 2 + sub
                        for h2 in range(2):
                            transpose(ptbank,
                                      pb[ptbank][:, sub * 256 + h2 * 128:
                                                 sub * 256 + (h2 + 1) * 128],
                                      prow_v[h2][:, tb * 128:(tb + 1) * 128],
                                      [prow_s[h2]])
                    dve(lambda e, p_=pp, b=ptbank: e.tensor_copy(ptb_v[p_][:, :],
                                                                 pb[b][:, 0:512]),
                        reads=[pslot[ptbank]], writes=[ptb_s[pp]])
                    for sub in range(2):
                        tb = gi * 2 + sub
                        mm2(3, pb[3][:, 0:256],
                            vb_v[:, tb * 128:(tb + 1) * 128],
                            ptb_v[pp][:, sub * 256:(sub + 1) * 256],
                            tb == 0, tb == nkv * 4 - 1, [vb_s, ptb_s[pp]])
                ap_ = qpar % 2
                dve(lambda e, p_=ap_, b_=bp: e.tensor_tensor(aot_v[p_][:, :],
                                                             pb[3][:, 0:256],
                                                             bc_v[b_][:, :], ALU.mult),
                    reads=[pslot[3], bc_s[bp]], writes=[aot_s[ap_]])
                dma_out(aos[h16 * 128:(h16 + 1) * 128, g * 256:(g + 1) * 256],
                        aot_v[ap_][:, :], [aot_s[ap_]])

    S.barrier()

    R4 = Region()
    aosl_v, aosl_s = [], []
    wpbl_v, wpbl_s = [], []
    for par in range(2):
        v, s = R4.get("aosl%d" % par, TOWN)
        aosl_v.append(v); aosl_s.append(s)
        v, s = R4.get("wpbl%d" % par, 512)
        wpbl_v.append(v); wpbl_s.append(s)
    xr_v, xr_s = R4.get("xr", 512)
    x1t_v, x1t_s = R4.get("x1t", 512)
    for e4 in range(4):
        for oc in range(NCC):
            p = oc % 2
            dma_in(aosl_v[p][:, :], aos[oc * 128:(oc + 1) * 128, :], [aosl_s[p]])
            dma_in(wpbl_v[p][:, :],
                   wp[oc * 128:(oc + 1) * 128, e4 * 512:(e4 + 1) * 512], [wpbl_s[p]])
            for tt in range(8):
                mm2(tt, pb[tt][:, 0:512],
                    aosl_v[p][:, tt * 128:(tt + 1) * 128],
                    wpbl_v[p][:, :], oc == 0, oc == NCC - 1,
                    [aosl_s[p], wpbl_s[p]])
        for tt in range(8):
            dma_in(xr_v[:, :], xown[tt * 128:(tt + 1) * 128, e4 * 512:(e4 + 1) * 512],
                   [xr_s])
            dve(lambda e, b=tt: e.tensor_tensor(x1t_v[:, :], pb[b][:, 0:512],
                                                xr_v[:, :], ALU.add),
                reads=[pslot[tt], xr_s], writes=[x1t_s])
            dma_out(x1s[tt * 128:(tt + 1) * 128, e4 * 512:(e4 + 1) * 512], x1t_v[:, :],
                    [x1t_s])

    S.barrier()

    R5 = Region()
    x1b_v, x1b_s = [], []
    for tt in range(2):
        v, s = R5.get("x1b%d" % tt, C)
        x1b_v.append(v); x1b_s.append(s)
    xn4_v, xn4_s = R5.get("xn4", C)
    xnt4_v, xnt4_s = R5.get("xnt4", 16 * 256)
    xnt4h_v, xnt4h_s = R5.get("xnt4h", 16 * 256, BF16)
    xnt4l_v, xnt4l_s = R5.get("xnt4l", 16 * 256, BF16)
    wgb_v, wgb_s = [], []
    wub_v, wub_s = [], []
    for par in range(2):
        v, s = R5.get("wgb%d" % par, C, BF16)
        wgb_v.append(v); wgb_s.append(s)
        v, s = R5.get("wub%d" % par, C, BF16)
        wub_v.append(v); wub_s.append(s)
    hb_v, hb_s = R5.get("hb", NIC * 256)
    hsg_v, hsg_s = [], []
    for par in range(2):
        v, s = R5.get("hsg%d" % par, 4 * 128, BF16)
        hsg_v.append(v); hsg_s.append(s)
    wdb_v, wdb_s = [], []
    for par in range(2):
        v, s = R5.get("wdb%d" % par, C, BF16)
        wdb_v.append(v); wdb_s.append(s)
    ga_v, ga_s = R5.get("gact", 256)
    ep_v, ep_s = R5.get("epi", 512)
    ot_v, ot_s = R5.get("otile", 512)

    for ft in range(4):
        i0 = ft * 256
        for j in range(2):
            dma_in(x1b_v[j][:, :], x1s[i0 + j * 128:i0 + (j + 1) * 128, :], [x1b_s[j]])
            ln_tile(R5, x1b_v[j], x1b_s[j], xn4_v, xn4_s)
            xpose_tile(R5, xn4_v, xn4_s, xnt4_v, xnt4_s, j, 256, b0=0)
        dve(lambda e: e.tensor_copy(xnt4h_v[:, :], xnt4_v[:, :]),
            reads=[xnt4_s], writes=[xnt4h_s])
        dve(lambda e: e.tensor_tensor(xnt4l_v[:, :], xnt4_v[:, :], xnt4h_v[:, :],
                                      ALU.subtract),
            reads=[xnt4_s, xnt4h_s], writes=[xnt4l_s])
        for ic in range(NIC):
            p = ic % 2
            dma_in_gp(wgb_v[p][:, :], wg[ic * 128:(ic + 1) * 128, :], [wgb_s[p]])
            dma_in_gp(wub_v[p][:, :], wu[ic * 128:(ic + 1) * 128, :], [wub_s[p]])
            for bk, wv_, ws_ in ((0, wgb_v[p], wgb_s[p]), (1, wub_v[p], wub_s[p])):
                for hl, (xv_, xs_) in enumerate(((xnt4h_v, xnt4h_s), (xnt4l_v, xnt4l_s))):
                    for cc in range(NCC):
                        mm2(bk, pb[bk][:, 0:256], wv_[:, cc * 128:(cc + 1) * 128],
                            xv_[:, cc * 256:(cc + 1) * 256],
                            hl == 0 and cc == 0, hl == 1 and cc == NCC - 1,
                            [ws_, xs_])
            act(lambda e, i=ic: e.activation(ga_v[:, :], pb[0][:, 0:256], AF.Silu,
                                             scale=p_gs[:, i:i + 1]),
                reads=[pslot[0], cslot["gs"]], writes=[ga_s])
            dve(lambda e, i=ic: e.scalar_tensor_tensor(hb_v[:, i * 256:(i + 1) * 256],
                                                       pb[1][:, 0:256], p_us[:, i:i + 1],
                                                       ga_v[:, :], ALU.mult, ALU.mult),
                reads=[pslot[1], cslot["us"], ga_s], writes=[hb_s])
        for ic in range(NIC):
            p = ic % 2
            dma_in_gp(wdb_v[p][:, :], wd[ic * 128:(ic + 1) * 128, :], [wdb_s[p]])
            for tt in range(2):
                hsl = hb_v[:, ic * 256 + tt * 128:ic * 256 + (tt + 1) * 128]
                dve(lambda e, o=hsg_v[p][:, tt * 128:(tt + 1) * 128], i=hsl:
                    e.tensor_copy(o, i),
                    reads=[hb_s], writes=[hsg_s[p]])
                dve(lambda e, o=hsg_v[p][:, (2 + tt) * 128:(3 + tt) * 128], i=hsl,
                    hi=hsg_v[p][:, tt * 128:(tt + 1) * 128]:
                    e.tensor_tensor(o, i, hi, ALU.subtract),
                    reads=[hb_s, hsg_s[p]], writes=[hsg_s[p]])
            for hl in range(2):
                for e4 in range(4):
                    for tt in range(2):
                        bk = e4 * 2 + tt
                        mm2(bk, pb[bk][:, 0:512],
                            hsg_v[p][:, (2 * hl + tt) * 128:(2 * hl + tt + 1) * 128],
                            wdb_v[p][:, e4 * 512:(e4 + 1) * 512],
                            ic == 0 and hl == 0,
                            ic == NIC - 1 and hl == 1,
                            [hsg_s[p], wdb_s[p]])
        for e4 in range(4):
            for tt in range(2):
                bk = e4 * 2 + tt
                dve(lambda e, b=bk, e4_=e4: e.tensor_tensor(ep_v[:, :], pb[b][:, 0:512],
                                                            p_ds[:, e4_ * 512:(e4_ + 1) * 512],
                                                            ALU.mult),
                    reads=[pslot[bk], cslot["ds"]], writes=[ep_s])
                dve(lambda e, t_=tt, e4_=e4: e.tensor_tensor(
                        ot_v[:, :], ep_v[:, :],
                        x1b_v[t_][:, e4_ * 512:(e4_ + 1) * 512], ALU.add),
                    reads=[ep_s, x1b_s[tt]], writes=[ot_s])
                dma_out(out[i0 + tt * 128:i0 + (tt + 1) * 128, e4 * 512:(e4 + 1) * 512],
                        ot_v[:, :], [ot_s])

    S.emit(nc)
    return nc


def _tile_stationary(wt, n_out):
    ci = wt.shape[0]
    a = wt.reshape(ci // 128, 128, n_out // 128, 128)
    return np.ascontiguousarray(a.transpose(2, 1, 0, 3).reshape(n_out, ci))


def _prep_shared(L0_attn_Wq, L0_attn_Wk, L0_attn_Wv, L0_attn_Wp,
                 L0_ffn_gate_w, L0_ffn_up_w, L0_ffn_down_w,
                 L0_ffn_gate_s, L0_ffn_up_s, L0_ffn_down_s):
    f = np.float32
    sh = {}
    sh["wq"] = _tile_stationary(np.ascontiguousarray(L0_attn_Wq.T, dtype=f), NQ * 128)
    sh["wk"] = _tile_stationary(np.ascontiguousarray(L0_attn_Wk.T, dtype=f), NKV * 128)
    sh["wv"] = np.ascontiguousarray(L0_attn_Wv.T, dtype=f)
    sh["wp"] = np.ascontiguousarray(L0_attn_Wp.T, dtype=f)
    sh["wg"] = _tile_stationary(np.ascontiguousarray(L0_ffn_gate_w.T, dtype=f), NIC * 128)
    sh["wu"] = _tile_stationary(np.ascontiguousarray(L0_ffn_up_w.T, dtype=f), NIC * 128)
    sh["wd"] = np.ascontiguousarray(L0_ffn_down_w.T, dtype=f)
    sh["gs"] = np.ascontiguousarray(np.asarray(L0_ffn_gate_s, dtype=f).reshape(NIC, 128).T)
    sh["us"] = np.ascontiguousarray(np.asarray(L0_ffn_up_s, dtype=f).reshape(NIC, 128).T)
    sh["dsb"] = np.ascontiguousarray(
        np.broadcast_to(np.asarray(L0_ffn_down_s, dtype=f), (128, C)))
    inv = (1.0 / ROPE_BASE ** (np.arange(64, dtype=np.float64) / 64.0))
    posk = np.arange(T, dtype=np.float64)
    angk = np.outer(inv, posk)
    sh["tabkc"] = np.concatenate([np.cos(angk), np.cos(angk)], 0).astype(f)
    sh["tabks"] = np.concatenate([-np.sin(angk), np.sin(angk)], 0).astype(f)
    rot = np.zeros((128, 128), dtype=f)
    rot[(np.arange(128) + 64) % 128, np.arange(128)] = 1.0
    sh["rotm"] = rot
    sh["idm"] = np.eye(128, dtype=f)
    sh["onesr"] = np.ones((1, 128), dtype=f)
    return sh


def _prep_core(x, r, sh):
    f = np.float32
    d = dict(sh)
    d["xall"] = np.ascontiguousarray(x, dtype=f)
    d["xown"] = np.ascontiguousarray(x[r::4, :], dtype=f)
    posq = (4.0 * np.arange(TOWN, dtype=np.float64) + r)
    inv = (1.0 / ROPE_BASE ** (np.arange(64, dtype=np.float64) / 64.0))
    angq = np.outer(inv, posq)
    d["tabqc"] = np.concatenate([np.cos(angq), np.cos(angq)], 0).astype(f)
    d["tabqs"] = np.concatenate([-np.sin(angq), np.sin(angq)], 0).astype(f)
    m = np.zeros((2, 128, TOWN), dtype=f)
    for h2 in range(2):
        qoff = 4 * (128 * h2 + np.arange(128)) + r
        kk = np.arange(TOWN)
        m[h2] = np.where(kk[None, :] <= qoff[:, None], 0.0, NEG)
    d["maskadd"] = m.reshape(256, TOWN)
    return d


_NC_CACHE = {}


def kernel(**inputs):
    x = np.asarray(inputs["L0_attn_x"], dtype=np.float32)
    sh = _prep_shared(
        np.asarray(inputs["L0_attn_Wq"]), np.asarray(inputs["L0_attn_Wk"]),
        np.asarray(inputs["L0_attn_Wv"]), np.asarray(inputs["L0_attn_Wp"]),
        np.asarray(inputs["L0_ffn_gate_w"]), np.asarray(inputs["L0_ffn_up_w"]),
        np.asarray(inputs["L0_ffn_down_w"]), np.asarray(inputs["L0_ffn_gate_s"]),
        np.asarray(inputs["L0_ffn_up_s"]), np.asarray(inputs["L0_ffn_down_s"]))
    in_maps = []
    for c in range(8):
        b, r = c // 4, c % 4
        in_maps.append(_prep_core(x[b], r, sh))
    if "nc" not in _NC_CACHE:
        _NC_CACHE["nc"] = build_program()
    nc = _NC_CACHE["nc"]
    res = run_bass_kernel_spmd(nc, in_maps, core_ids=list(range(8)))
    full = np.empty((B, T, C), dtype=np.float32)
    for c in range(8):
        b, r = c // 4, c % 4
        full[b, r::4, :] = res.results[c]["out"]
    return full


# version 05
# speedup vs baseline: 1.0649x; 1.0649x over previous
import math
import numpy as np
import concourse.bass as bass
import concourse.mybir as mybir
from concourse.bass_utils import run_bass_kernel_spmd

F32 = mybir.dt.float32
BF16 = mybir.dt.bfloat16
AF = mybir.ActivationFunctionType
ALU = mybir.AluOpType
AX = mybir.AxisListType

B, T, C = 2, 4096, 2048
NQ, NKV, HD = 16, 4, 128
INTER = 5632
NCC = C // 128
NIC = INTER // 128
TOWN = 1024
NST = T // 512
EPS = 1e-5
ASCALE = 1.0 / math.sqrt(HD)
ROPE_BASE = 10000.0
NEG = -1.0e30
ARENA_COLS = 34048
BARENA_COLS = 28672

ENGS = ("pe", "act", "dve", "sp", "gp")


class Slot:
    __slots__ = ("w", "rs")

    def __init__(self):
        self.w = None
        self.rs = []

    def rdeps(self):
        return [self.w] if self.w else []

    def wdeps(self):
        d = list(self.rs)
        if self.w:
            d.append(self.w)
        return d


class Sched:
    NQSEM = 8
    SELF_WINDOW = 8
    SELF_SYNC_ENGS = ("act", "dve")

    def __init__(self):
        self.sem_names = ["pe", "act", "dve"]
        for q in ("sp", "gp"):
            self.sem_names += ["%s%d" % (q, i) for i in range(self.NQSEM)]
        self.ops = {e: [] for e in ENGS}
        self.cnt = {s: 0 for s in self.sem_names}
        self.ecnt = {e: 0 for e in ENGS}
        self.dma_i = {"sp": 0, "gp": 0}
        self.pending_barrier = {}

    def barrier(self):
        snap = {s: v for s, v in self.cnt.items() if v > 0}
        for e in ENGS:
            old = self.pending_barrier.get(e, {})
            for s, v in snap.items():
                if old.get(s, 0) < v:
                    old[s] = v
            self.pending_barrier[e] = old

    def op(self, eng, fn, reads=(), writes=(), inc=None):
        deps = []
        for s in reads:
            deps += s.rdeps()
        for s in writes:
            deps += s.wdeps()
        need = self.pending_barrier.pop(eng, {})
        own = eng if eng in ("pe", "act", "dve") else None
        for (sem, v) in deps:
            if sem == own:
                if (eng in self.SELF_SYNC_ENGS and v > self.cnt[eng] - self.SELF_WINDOW
                        and need.get(sem, 0) < v):
                    need[sem] = v
            elif eng == "pe" and sem == "pe":
                pass
            else:
                if need.get(sem, 0) < v:
                    need[sem] = v
        if eng in ("sp", "gp"):
            sem = "%s%d" % (eng, self.dma_i[eng] % self.NQSEM)
            self.dma_i[eng] += 1
            step = 16
        else:
            sem = eng
            step = 1 if inc is None else inc
        self.cnt[sem] += step
        tok = (sem, self.cnt[sem])
        self.ops[eng].append((fn, need, sem, step))
        self.ecnt[eng] += 1
        for s in reads:
            s.rs.append(tok)
        for s in writes:
            s.w = tok
            s.rs = []
        return tok

    def emit(self, nc):
        sems = {s: nc.alloc_semaphore("sem_" + s) for s in self.sem_names}
        final_gp = [( "gp%d" % i, self.cnt["gp%d" % i]) for i in range(self.NQSEM)]
        with nc.Block() as block:
            @block.tensor
            def _(engine):
                self._run(engine, "pe", sems)
            @block.scalar
            def _(engine):
                self._run(engine, "act", sems)
            @block.vector
            def _(engine):
                self._run(engine, "dve", sems)
            @block.sync
            def _(engine):
                self._run(engine, "sp", sems)
            @block.gpsimd
            def _(engine):
                self._run(engine, "gp", sems)
                for s, v in final_gp:
                    if v > 0:
                        engine.wait_ge(sems[s], v)

    def _run(self, engine, engname, sems):
        water = {s: 0 for s in self.sem_names}
        for (fn, need, sem, step) in self.ops[engname]:
            for s, v in sorted(need.items()):
                if water[s] < v:
                    engine.wait_ge(sems[s], v)
                    water[s] = v
            ins = fn(engine)
            ins.then_inc(sems[sem], step)


def build_program(debug=False):
    nc = bass.Bass("TRN2", target_bir_lowering=False)
    S = Sched()

    def din(name, shape):
        return nc.dram_tensor(name, shape, F32, kind="ExternalInput").ap()

    xall = din("xall", [T, C])
    xown = din("xown", [TOWN, C])
    wq = din("wq", [NQ * 128, C])
    wk = din("wk", [NKV * 128, C])
    wv = din("wv", [NKV * 128, C])
    wp = din("wp", [C, C])
    wg = din("wg", [NIC * 128, C])
    wu = din("wu", [NIC * 128, C])
    wd = din("wd", [INTER, C])
    gs = din("gs", [128, NIC])
    us = din("us", [128, NIC])
    dsb = din("dsb", [128, C])
    tabkc = din("tabkc", [128, T])
    tabks = din("tabks", [128, T])
    tabqc = din("tabqc", [128, TOWN])
    tabqs = din("tabqs", [128, TOWN])
    rotm = din("rotm", [128, 128])
    idm = din("idm", [128, 128])
    onesr = din("onesr", [1, 128])
    maskadd = din("maskadd", [2 * 128, TOWN])

    out = nc.dram_tensor("out", [TOWN, C], F32, kind="ExternalOutput").ap()
    def dscratch(name, shape):
        if debug:
            return nc.dram_tensor(name, shape, F32, kind="ExternalOutput").ap()
        return nc.dram_tensor(name, shape, F32).ap()
    ks = dscratch("ks", [NKV * 128, T])
    vs = dscratch("vs", [NKV * 128, T])
    qs = dscratch("qs", [NQ * 128, TOWN])
    aos = dscratch("aos", [NQ * 128, TOWN])
    x1s = dscratch("x1s", [TOWN, C])

    arena = nc.alloc_sbuf_tensor("arena", [128, ARENA_COLS], F32)
    arenab = nc.alloc_sbuf_tensor("arenab", [128, BARENA_COLS], BF16)
    p_rotm = nc.alloc_sbuf_tensor("p_rotm", [128, 128], F32)
    p_idm = nc.alloc_sbuf_tensor("p_idm", [128, 128], F32)
    p_mask = nc.alloc_sbuf_tensor("p_mask", [128, 2 * TOWN], F32)
    p_gs = nc.alloc_sbuf_tensor("p_gs", [128, NIC], F32)
    p_us = nc.alloc_sbuf_tensor("p_us", [128, NIC], F32)
    p_ds = nc.alloc_sbuf_tensor("p_ds", [128, C], F32)
    p_ones = nc.alloc_sbuf_tensor("p_ones", [1, 128], F32)

    pb = [nc.alloc_psum_tensor("pb%d" % i, [128, 512], F32) for i in range(8)]
    pslot = [Slot() for _ in range(8)]

    class Region:
        def __init__(self):
            self.off = 0
            self.boff = 0
            self.views = {}
            self.slots = {}

        def get(self, name, cols, dt=F32):
            if name not in self.views:
                if dt is F32:
                    assert self.off + cols <= ARENA_COLS, (name, self.off, cols)
                    self.views[name] = arena[:, self.off:self.off + cols]
                    self.off += cols
                else:
                    assert self.boff + cols <= BARENA_COLS, (name, self.boff, cols)
                    self.views[name] = arenab[:, self.boff:self.boff + cols]
                    self.boff += cols
                self.slots[name] = Slot()
            return self.views[name], self.slots[name]

    cslot = {n: Slot() for n in ("rotm", "idm", "mask", "gs", "us", "ds", "ones")}

    def dma_in(dst_ap, src_ap, writes, reads=()):
        return S.op("sp", lambda e, o=dst_ap, i=src_ap: e.dma_start(out=o, in_=i),
                    reads=reads, writes=writes)

    def dma_in_gp(dst_ap, src_ap, writes, reads=()):
        return S.op("gp", lambda e, o=dst_ap, i=src_ap: e.dma_start(out=o, in_=i),
                    reads=reads, writes=writes)

    def dma_out(dst_ap, src_ap, reads):
        return S.op("gp", lambda e, o=dst_ap, i=src_ap: e.dma_start(out=o, in_=i),
                    reads=reads)

    def mm2(bank, out_ap, lhsT, rhs, start, stop, reads):
        return S.op("pe",
                    lambda e, o=out_ap, l=lhsT, rr=rhs, s1=start, s2=stop:
                        e.matmul(o, l, rr, start=s1, stop=s2),
                    reads=list(reads), writes=[pslot[bank]])

    def transpose(bank, out_ap, in_ap, reads):
        return S.op("pe",
                    lambda e, o=out_ap, i=in_ap: e.transpose(o, i, p_idm[:, :]),
                    reads=list(reads) + [cslot["idm"]], writes=[pslot[bank]])

    def act(fn, reads, writes):
        return S.op("act", fn, reads=reads, writes=writes)

    def dve(fn, reads, writes):
        return S.op("dve", fn, reads=reads, writes=writes)

    def ln_tile(R, xv, xslot, xnv, xnslot):
        sq, sq_s = R.get("ln_sq", C)
        st, st_s = R.get("ln_st", 16)
        act(lambda e: e.activation(sq[:, :], xv[:, :], AF.Square,
                                   accum_out=st[:, 2:3]),
            reads=[xslot], writes=[sq_s, st_s])
        dve(lambda e: e.reduce_sum(st[:, 0:1], xv[:, :], axis=AX.X),
            reads=[xslot], writes=[st_s])
        dve(lambda e: e.tensor_scalar(st[:, 1:2], st[:, 0:1], 1.0 / C, None, ALU.mult),
            reads=[st_s], writes=[st_s])
        dve(lambda e: e.tensor_scalar(st[:, 3:4], st[:, 2:3], 1.0 / C, None, ALU.mult),
            reads=[st_s], writes=[st_s])
        dve(lambda e: e.tensor_tensor(st[:, 4:5], st[:, 1:2], st[:, 1:2], ALU.mult),
            reads=[st_s], writes=[st_s])
        dve(lambda e: e.tensor_tensor(st[:, 5:6], st[:, 3:4], st[:, 4:5], ALU.subtract),
            reads=[st_s], writes=[st_s])
        dve(lambda e: e.tensor_scalar(st[:, 10:11], st[:, 5:6], EPS, None, ALU.add),
            reads=[st_s], writes=[st_s])
        act(lambda e: e.activation(st[:, 6:7], st[:, 10:11], AF.Sqrt),
            reads=[st_s], writes=[st_s])
        dve(lambda e: e.reciprocal(st[:, 7:8], st[:, 6:7]),
            reads=[st_s], writes=[st_s])
        dve(lambda e: e.tensor_tensor(st[:, 8:9], st[:, 1:2], st[:, 7:8], ALU.mult),
            reads=[st_s], writes=[st_s])
        dve(lambda e: e.tensor_scalar(st[:, 9:10], st[:, 8:9], -1.0, None, ALU.mult),
            reads=[st_s], writes=[st_s])
        dve(lambda e: e.tensor_scalar(xnv[:, :], xv[:, :], st[:, 7:8], st[:, 9:10],
                                      ALU.mult, ALU.add),
            reads=[xslot, st_s], writes=[xnslot])

    def xpose_tile(R, xnv, xnslot, xntv, xntslot, j, tw, b0=0):
        for grp in range(4):
            for q in range(4):
                cc = grp * 4 + q
                transpose(b0 + grp, pb[b0 + grp][:, q * 128:(q + 1) * 128],
                          xnv[:, cc * 128:(cc + 1) * 128], reads=[xnslot])
            outap = xntv.rearrange("p (c t) -> p c t", t=tw)[:, grp * 4:grp * 4 + 4,
                                                            j * 128:(j + 1) * 128]
            inap = pb[b0 + grp][:, 0:512].rearrange("p (c t) -> p c t", t=128)
            dve(lambda e, o=outap, i=inap: e.tensor_copy(o, i),
                reads=[pslot[b0 + grp]], writes=[xntslot])

    dma_in(p_rotm[:, :], rotm, [cslot["rotm"]])
    dma_in(p_idm[:, :], idm, [cslot["idm"]])
    dma_in(p_mask[:, 0:TOWN], maskadd[0:128, :], [cslot["mask"]])
    dma_in(p_mask[:, TOWN:2 * TOWN], maskadd[128:256, :], [cslot["mask"]])
    dma_in(p_gs[:, :], gs, [cslot["gs"]])
    dma_in(p_us[:, :], us, [cslot["us"]])
    dma_in(p_ds[:, :], dsb, [cslot["ds"]])
    dma_in(p_ones[:, :], onesr, [cslot["ones"]])

    R1 = Region()
    tkc_v, tks_v, tkc_sl, tks_sl = [], [], [], []
    for par in range(2):
        v, s = R1.get("tabkc%d" % par, 512)
        tkc_v.append(v); tkc_sl.append(s)
        v, s = R1.get("tabks%d" % par, 512)
        tks_v.append(v); tks_sl.append(s)
    wkv_v, wkv_s = R1.get("wk", NKV * C)
    for h in range(NKV):
        dma_in(wkv_v[:, h * C:(h + 1) * C], wk[h * 128:(h + 1) * 128, :], [wkv_s])
    xb_v, xb_s = [], []
    for par in range(2):
        v, s = R1.get("xb%d" % par, C)
        xb_v.append(v); xb_s.append(s)
    xn_v, xn_s = R1.get("xn", C)
    xnt_v, xnt_s = R1.get("xnt", 16 * 512)
    wvb_v, wvb_s = [], []
    for par in range(2):
        v, s = R1.get("wvb%d" % par, C)
        wvb_v.append(v); wvb_s.append(s)
    kraw_v, kraw_s = R1.get("kraw", 512)
    kst_v, kst_s = R1.get("kst", 512)
    ktm_v, ktm_s = R1.get("ktm", 512)
    kfin_v, kfin_s = R1.get("kfin", 512)
    vst_v, vst_s = R1.get("vst", 512)

    for st in range(NST):
        t0 = st * 512
        tp = st % 2
        tkc, tkc_s = tkc_v[tp], tkc_sl[tp]
        tks, tks_s = tks_v[tp], tks_sl[tp]
        dma_in(tkc[:, :], tabkc[:, t0:t0 + 512], [tkc_s])
        dma_in(tks[:, :], tabks[:, t0:t0 + 512], [tks_s])
        for j in range(4):
            p = j % 2
            dma_in(xb_v[p][:, :], xall[t0 + j * 128:t0 + (j + 1) * 128, :], [xb_s[p]])
            ln_tile(R1, xb_v[p], xb_s[p], xn_v, xn_s)
            xpose_tile(R1, xn_v, xn_s, xnt_v, xnt_s, j, 512, b0=0)
        for h in range(NKV):
            bv = h % 2
            p = h % 2
            dma_in(wvb_v[p][:, :], wv[h * 128:(h + 1) * 128, :], [wvb_s[p]])
            for cc in range(NCC):
                mm2(bv, pb[bv][:, 0:512], wvb_v[p][:, cc * 128:(cc + 1) * 128],
                    xnt_v[:, cc * 512:(cc + 1) * 512], cc == 0, cc == NCC - 1,
                    [wvb_s[p], xnt_s])
            act(lambda e, b=bv: e.copy(vst_v[:, :], pb[b][:, 0:512]),
                reads=[pslot[bv]], writes=[vst_s])
            dma_out(vs[h * 128:(h + 1) * 128, t0:t0 + 512], vst_v[:, :], [vst_s])
        for h in range(NKV):
            bk = 4 + (h % 2)
            br = 6 + (h % 2)
            for cc in range(NCC):
                mm2(bk, pb[bk][:, 0:512], wkv_v[:, h * C + cc * 128:h * C + (cc + 1) * 128],
                    xnt_v[:, cc * 512:(cc + 1) * 512], cc == 0, cc == NCC - 1,
                    [wkv_s, xnt_s])
            act(lambda e, b=bk: e.copy(kraw_v[:, :], pb[b][:, 0:512]),
                reads=[pslot[bk]], writes=[kraw_s])
            mm2(br, pb[br][:, 0:512], p_rotm[:, :], kraw_v[:, :], True, True,
                [cslot["rotm"], kraw_s])
            dve(lambda e, tv=tkc: e.tensor_tensor(kst_v[:, :], kraw_v[:, :],
                                                  tv[:, :], ALU.mult),
                reads=[kraw_s, tkc_s], writes=[kst_s])
            dve(lambda e, b=br, tv=tks: e.tensor_tensor(ktm_v[:, :], pb[b][:, 0:512],
                                                        tv[:, :], ALU.mult),
                reads=[pslot[br], tks_s], writes=[ktm_s])
            dve(lambda e: e.tensor_tensor(kfin_v[:, :], kst_v[:, :], ktm_v[:, :], ALU.add),
                reads=[kst_s, ktm_s], writes=[kfin_s])
            dma_out(ks[h * 128:(h + 1) * 128, t0:t0 + 512], kfin_v[:, :], [kfin_s])

    S.barrier()

    R2 = Region()
    tqc, tqc_s = R2.get("tabqc", TOWN)
    tqs, tqs_s = R2.get("tabqs", TOWN)
    dma_in(tqc[:, :], tabqc, [tqc_s])
    dma_in(tqs[:, :], tabqs, [tqs_s])
    xb2_v, xb2_s = [], []
    for par in range(2):
        v, s = R2.get("xb%d" % par, C)
        xb2_v.append(v); xb2_s.append(s)
    xn2_v, xn2_s = R2.get("xn", C)
    xnt2_v, xnt2_s = [], []
    for qt in range(2):
        v, s = R2.get("xnt%d" % qt, 16 * 512)
        xnt2_v.append(v); xnt2_s.append(s)
    wqb_v, wqb_s = [], []
    for par in range(2):
        v, s = R2.get("wqb%d" % par, C)
        wqb_v.append(v); wqb_s.append(s)
    qraw_v, qraw_s = R2.get("qraw", 512)
    qst_v, qst_s = R2.get("qst", 512)
    qtm_v, qtm_s = R2.get("qtm", 512)
    qfin_v, qfin_s = R2.get("qfin", 512)

    for qt in range(2):
        for j in range(4):
            p = j % 2
            dma_in(xb2_v[p][:, :], xown[qt * 512 + j * 128:qt * 512 + (j + 1) * 128, :],
                   [xb2_s[p]])
            ln_tile(R2, xb2_v[p], xb2_s[p], xn2_v, xn2_s)
            xpose_tile(R2, xn2_v, xn2_s, xnt2_v[qt], xnt2_s[qt], j, 512, b0=0)
    for oc in range(NQ):
        p = oc % 2
        dma_in(wqb_v[p][:, :], wq[oc * 128:(oc + 1) * 128, :], [wqb_s[p]])
        for qt in range(2):
            bk = 4 + (qt % 2)
            br = 6 + (qt % 2)
            t0 = qt * 512
            for cc in range(NCC):
                mm2(bk, pb[bk][:, 0:512], wqb_v[p][:, cc * 128:(cc + 1) * 128],
                    xnt2_v[qt][:, cc * 512:(cc + 1) * 512], cc == 0, cc == NCC - 1,
                    [wqb_s[p], xnt2_s[qt]])
            act(lambda e, b=bk: e.copy(qraw_v[:, :], pb[b][:, 0:512]),
                reads=[pslot[bk]], writes=[qraw_s])
            mm2(br, pb[br][:, 0:512], p_rotm[:, :], qraw_v[:, :], True, True,
                [cslot["rotm"], qraw_s])
            dve(lambda e, t_=t0: e.tensor_tensor(qst_v[:, :], qraw_v[:, :],
                                                 tqc[:, t_:t_ + 512], ALU.mult),
                reads=[qraw_s, tqc_s], writes=[qst_s])
            dve(lambda e, b=br, t_=t0: e.tensor_tensor(qtm_v[:, :], pb[b][:, 0:512],
                                                       tqs[:, t_:t_ + 512], ALU.mult),
                reads=[pslot[br], tqs_s], writes=[qtm_s])
            dve(lambda e: e.tensor_tensor(qfin_v[:, :], qst_v[:, :], qtm_v[:, :], ALU.add),
                reads=[qst_s, qtm_s], writes=[qfin_s])
            dma_out(qs[oc * 128:(oc + 1) * 128, t0:t0 + 512], qfin_v[:, :], [qfin_s])

    S.barrier()

    R3 = Region()
    qtb_v, qtb_s = [], []
    for par in range(2):
        v, s = R3.get("qtb%d" % par, 256)
        qtb_v.append(v); qtb_s.append(s)
    kb_v, kb_s = R3.get("kb", T)
    vb_v, vb_s = R3.get("vb", T)
    srow_v, srow_s, prow_v, prow_s = [], [], [], []
    for h2 in range(2):
        v, s = R3.get("srow%d" % h2, T)
        srow_v.append(v); srow_s.append(s)
        v, s = R3.get("prow%d" % h2, T)
        prow_v.append(v); prow_s.append(s)
    strip_v, strip_s = R3.get("strip", 32)
    sst_v, sst_s = R3.get("sst", 16)
    vfm_v, vfm_s = R3.get("vfm", 1024)
    ptb_v, ptb_s = [], []
    for par in range(2):
        v, s = R3.get("ptb%d" % par, 512)
        ptb_v.append(v); ptb_s.append(s)
    aot_v, aot_s = [], []
    rr_v, rr_s, bc_v, bc_s = [], [], [], []
    for par in range(2):
        v, s = R3.get("aot%d" % par, 256)
        aot_v.append(v); aot_s.append(s)
        v, s = R3.get("rrow%d" % par, 256)
        rr_v.append(v); rr_s.append(s)
        v, s = R3.get("bc%d" % par, 256)
        bc_v.append(v); bc_s.append(s)

    qpar = 0
    for k in range(NKV):
        for g in range(4):
            kvlen = (g + 1) * 1024
            nkv = kvlen // 512
            dma_in(kb_v[:, g * 1024:(g + 1) * 1024],
                   ks[k * 128:(k + 1) * 128, g * 1024:(g + 1) * 1024], [kb_s])
            dma_in(vfm_v[:, :],
                   vs[k * 128:(k + 1) * 128, g * 1024:(g + 1) * 1024], [vfm_s])
            for vg in range(2):
                vbank = 2 if vg == 0 else 6
                for sub in range(4):
                    transpose(vbank, pb[vbank][:, sub * 128:(sub + 1) * 128],
                              vfm_v[:, (vg * 4 + sub) * 128:(vg * 4 + sub + 1) * 128],
                              [vfm_s])
                dve(lambda e, b=vbank, o=vb_v[:, g * 1024 + vg * 512:
                                              g * 1024 + (vg + 1) * 512]:
                    e.tensor_copy(o, pb[b][:, 0:512]),
                    reads=[pslot[vbank]], writes=[vb_s])
            for qi in range(4):
                h16 = k * 4 + qi
                qp = qpar % 2
                qpar += 1
                dma_in(qtb_v[qp][:, :],
                       qs[h16 * 128:(h16 + 1) * 128, g * 256:(g + 1) * 256],
                       [qtb_s[qp]])
                for h2 in range(2):
                    qtile = qtb_v[qp][:, h2 * 128:(h2 + 1) * 128]
                    sr, sr_s = srow_v[h2], srow_s[h2]
                    pr, pr_s = prow_v[h2], prow_s[h2]
                    for t in range(nkv):
                        bk = 4 * h2 + t % 2
                        mm2(bk, pb[bk][:, 0:512], qtile, kb_v[:, t * 512:(t + 1) * 512],
                            True, True, [qtb_s[qp], kb_s])
                        if t >= nkv - 2:
                            moff = h2 * TOWN + (t - (nkv - 2)) * 512
                            dve(lambda e, b=bk, tt=t, mo=moff, o=sr:
                                e.tensor_tensor(o[:, tt * 512:(tt + 1) * 512],
                                                pb[b][:, 0:512],
                                                p_mask[:, mo:mo + 512], ALU.add),
                                reads=[pslot[bk], cslot["mask"]], writes=[sr_s])
                        else:
                            act(lambda e, b=bk, tt=t, o=sr:
                                e.copy(o[:, tt * 512:(tt + 1) * 512], pb[b][:, 0:512]),
                                reads=[pslot[bk]], writes=[sr_s])
                        dve(lambda e, tt=t, i=sr, c=h2 * 8 + t:
                            e.reduce_max(strip_v[:, c:c + 1],
                                         i[:, tt * 512:(tt + 1) * 512], axis=AX.X),
                            reads=[sr_s], writes=[strip_s])
                    c0 = h2 * 4
                    dve(lambda e, n=nkv, c=h2 * 8, c0_=c0:
                        e.reduce_max(sst_v[:, c0_:c0_ + 1], strip_v[:, c:c + n], axis=AX.X),
                        reads=[strip_s], writes=[sst_s])
                    dve(lambda e, c0_=c0: e.tensor_scalar(sst_v[:, c0_ + 1:c0_ + 2],
                                                          sst_v[:, c0_:c0_ + 1],
                                                          -ASCALE, None, ALU.mult),
                        reads=[sst_s], writes=[sst_s])
                    act(lambda e, n=nkv, i=sr, o=pr, c0_=c0:
                        e.activation(o[:, 0:n * 512], i[:, 0:n * 512],
                                     AF.Exp, bias=sst_v[:, c0_ + 1:c0_ + 2], scale=ASCALE,
                                     accum_out=sst_v[:, c0_ + 2:c0_ + 3]),
                        reads=[sr_s, sst_s], writes=[pr_s, sst_s])
                    dve(lambda e, c0_=c0: e.reciprocal(sst_v[:, c0_ + 3:c0_ + 4],
                                                       sst_v[:, c0_ + 2:c0_ + 3]),
                        reads=[sst_s], writes=[sst_s])
                bp = qpar % 2
                for gi in range(nkv * 2):
                    pp = gi % 2
                    ptbank = 2 if pp == 0 else 6
                    for sub in range(2):
                        tb = gi * 2 + sub
                        for h2 in range(2):
                            transpose(ptbank,
                                      pb[ptbank][:, sub * 256 + h2 * 128:
                                                 sub * 256 + (h2 + 1) * 128],
                                      prow_v[h2][:, tb * 128:(tb + 1) * 128],
                                      [prow_s[h2]])
                    dve(lambda e, p_=pp, b=ptbank: e.tensor_copy(ptb_v[p_][:, :],
                                                                 pb[b][:, 0:512]),
                        reads=[pslot[ptbank]], writes=[ptb_s[pp]])
                    for sub in range(2):
                        tb = gi * 2 + sub
                        mm2(3, pb[3][:, 0:256],
                            vb_v[:, tb * 128:(tb + 1) * 128],
                            ptb_v[pp][:, sub * 256:(sub + 1) * 256],
                            tb == 0, tb == nkv * 4 - 1, [vb_s, ptb_s[pp]])
                for h2 in range(2):
                    transpose(7, pb[7][0:1, h2 * 128:(h2 + 1) * 128],
                              sst_v[:, h2 * 4 + 3:h2 * 4 + 4], [sst_s])
                dve(lambda e, p_=bp: e.tensor_copy(rr_v[p_][0:1, :], pb[7][0:1, 0:256]),
                    reads=[pslot[7]], writes=[rr_s[bp]])
                mm2(7, pb[7][:, 256:512], p_ones[0:1, :], rr_v[bp][0:1, :],
                    True, True, [cslot["ones"], rr_s[bp]])
                dve(lambda e, p_=bp: e.tensor_copy(bc_v[p_][:, :], pb[7][:, 256:512]),
                    reads=[pslot[7]], writes=[bc_s[bp]])
                ap_ = qpar % 2
                dve(lambda e, p_=ap_, b_=bp: e.tensor_tensor(aot_v[p_][:, :],
                                                             pb[3][:, 0:256],
                                                             bc_v[b_][:, :], ALU.mult),
                    reads=[pslot[3], bc_s[bp]], writes=[aot_s[ap_]])
                dma_out(aos[h16 * 128:(h16 + 1) * 128, g * 256:(g + 1) * 256],
                        aot_v[ap_][:, :], [aot_s[ap_]])

    S.barrier()

    R4 = Region()
    aosl_v, aosl_s = [], []
    wpbl_v, wpbl_s = [], []
    for par in range(2):
        v, s = R4.get("aosl%d" % par, TOWN)
        aosl_v.append(v); aosl_s.append(s)
        v, s = R4.get("wpbl%d" % par, 512)
        wpbl_v.append(v); wpbl_s.append(s)
    xr_v, xr_s = R4.get("xr", 512)
    x1t_v, x1t_s = R4.get("x1t", 512)
    for e4 in range(4):
        for oc in range(NCC):
            p = oc % 2
            dma_in(aosl_v[p][:, :], aos[oc * 128:(oc + 1) * 128, :], [aosl_s[p]])
            dma_in(wpbl_v[p][:, :],
                   wp[oc * 128:(oc + 1) * 128, e4 * 512:(e4 + 1) * 512], [wpbl_s[p]])
            for tt in range(8):
                mm2(tt, pb[tt][:, 0:512],
                    aosl_v[p][:, tt * 128:(tt + 1) * 128],
                    wpbl_v[p][:, :], oc == 0, oc == NCC - 1,
                    [aosl_s[p], wpbl_s[p]])
        for tt in range(8):
            dma_in(xr_v[:, :], xown[tt * 128:(tt + 1) * 128, e4 * 512:(e4 + 1) * 512],
                   [xr_s])
            dve(lambda e, b=tt: e.tensor_tensor(x1t_v[:, :], pb[b][:, 0:512],
                                                xr_v[:, :], ALU.add),
                reads=[pslot[tt], xr_s], writes=[x1t_s])
            dma_out(x1s[tt * 128:(tt + 1) * 128, e4 * 512:(e4 + 1) * 512], x1t_v[:, :],
                    [x1t_s])

    S.barrier()

    R5 = Region()
    x1b_v, x1b_s = [], []
    for tt in range(2):
        v, s = R5.get("x1b%d" % tt, C)
        x1b_v.append(v); x1b_s.append(s)
    xn4_v, xn4_s = R5.get("xn4", C)
    xnt4_v, xnt4_s = R5.get("xnt4", 16 * 256)
    xnt4h_v, xnt4h_s = R5.get("xnt4h", 16 * 256, BF16)
    xnt4l_v, xnt4l_s = R5.get("xnt4l", 16 * 256, BF16)
    wgb_v, wgb_s = [], []
    wub_v, wub_s = [], []
    for par in range(2):
        v, s = R5.get("wgb%d" % par, C, BF16)
        wgb_v.append(v); wgb_s.append(s)
        v, s = R5.get("wub%d" % par, C, BF16)
        wub_v.append(v); wub_s.append(s)
    hb_v, hb_s = R5.get("hb", NIC * 256)
    hsg_v, hsg_s = [], []
    for par in range(2):
        v, s = R5.get("hsg%d" % par, 4 * 128, BF16)
        hsg_v.append(v); hsg_s.append(s)
    wdb_v, wdb_s = [], []
    for par in range(2):
        v, s = R5.get("wdb%d" % par, C, BF16)
        wdb_v.append(v); wdb_s.append(s)
    ga_v, ga_s = R5.get("gact", 256)
    ep_v, ep_s = R5.get("epi", 512)
    ot_v, ot_s = R5.get("otile", 512)

    for ft in range(4):
        i0 = ft * 256
        for j in range(2):
            dma_in(x1b_v[j][:, :], x1s[i0 + j * 128:i0 + (j + 1) * 128, :], [x1b_s[j]])
            ln_tile(R5, x1b_v[j], x1b_s[j], xn4_v, xn4_s)
            xpose_tile(R5, xn4_v, xn4_s, xnt4_v, xnt4_s, j, 256, b0=0)
        dve(lambda e: e.tensor_copy(xnt4h_v[:, :], xnt4_v[:, :]),
            reads=[xnt4_s], writes=[xnt4h_s])
        dve(lambda e: e.tensor_tensor(xnt4l_v[:, :], xnt4_v[:, :], xnt4h_v[:, :],
                                      ALU.subtract),
            reads=[xnt4_s, xnt4h_s], writes=[xnt4l_s])
        for ic in range(NIC):
            p = ic % 2
            dma_in_gp(wgb_v[p][:, :], wg[ic * 128:(ic + 1) * 128, :], [wgb_s[p]])
            dma_in_gp(wub_v[p][:, :], wu[ic * 128:(ic + 1) * 128, :], [wub_s[p]])
            for bk, wv_, ws_ in ((0, wgb_v[p], wgb_s[p]), (1, wub_v[p], wub_s[p])):
                for hl, (xv_, xs_) in enumerate(((xnt4h_v, xnt4h_s), (xnt4l_v, xnt4l_s))):
                    for cc in range(NCC):
                        mm2(bk, pb[bk][:, 0:256], wv_[:, cc * 128:(cc + 1) * 128],
                            xv_[:, cc * 256:(cc + 1) * 256],
                            hl == 0 and cc == 0, hl == 1 and cc == NCC - 1,
                            [ws_, xs_])
            act(lambda e, i=ic: e.activation(ga_v[:, :], pb[0][:, 0:256], AF.Silu,
                                             scale=p_gs[:, i:i + 1]),
                reads=[pslot[0], cslot["gs"]], writes=[ga_s])
            dve(lambda e, i=ic: e.scalar_tensor_tensor(hb_v[:, i * 256:(i + 1) * 256],
                                                       pb[1][:, 0:256], p_us[:, i:i + 1],
                                                       ga_v[:, :], ALU.mult, ALU.mult),
                reads=[pslot[1], cslot["us"], ga_s], writes=[hb_s])
        for ic in range(NIC):
            p = ic % 2
            dma_in_gp(wdb_v[p][:, :], wd[ic * 128:(ic + 1) * 128, :], [wdb_s[p]])
            for tt in range(2):
                hsl = hb_v[:, ic * 256 + tt * 128:ic * 256 + (tt + 1) * 128]
                dve(lambda e, o=hsg_v[p][:, tt * 128:(tt + 1) * 128], i=hsl:
                    e.tensor_copy(o, i),
                    reads=[hb_s], writes=[hsg_s[p]])
                dve(lambda e, o=hsg_v[p][:, (2 + tt) * 128:(3 + tt) * 128], i=hsl,
                    hi=hsg_v[p][:, tt * 128:(tt + 1) * 128]:
                    e.tensor_tensor(o, i, hi, ALU.subtract),
                    reads=[hb_s, hsg_s[p]], writes=[hsg_s[p]])
            for hl in range(2):
                for e4 in range(4):
                    for tt in range(2):
                        bk = e4 * 2 + tt
                        mm2(bk, pb[bk][:, 0:512],
                            hsg_v[p][:, (2 * hl + tt) * 128:(2 * hl + tt + 1) * 128],
                            wdb_v[p][:, e4 * 512:(e4 + 1) * 512],
                            ic == 0 and hl == 0,
                            ic == NIC - 1 and hl == 1,
                            [hsg_s[p], wdb_s[p]])
        for e4 in range(4):
            for tt in range(2):
                bk = e4 * 2 + tt
                dve(lambda e, b=bk, e4_=e4: e.tensor_tensor(ep_v[:, :], pb[b][:, 0:512],
                                                            p_ds[:, e4_ * 512:(e4_ + 1) * 512],
                                                            ALU.mult),
                    reads=[pslot[bk], cslot["ds"]], writes=[ep_s])
                dve(lambda e, t_=tt, e4_=e4: e.tensor_tensor(
                        ot_v[:, :], ep_v[:, :],
                        x1b_v[t_][:, e4_ * 512:(e4_ + 1) * 512], ALU.add),
                    reads=[ep_s, x1b_s[tt]], writes=[ot_s])
                dma_out(out[i0 + tt * 128:i0 + (tt + 1) * 128, e4 * 512:(e4 + 1) * 512],
                        ot_v[:, :], [ot_s])

    S.emit(nc)
    return nc


def _tile_stationary(wt, n_out):
    ci = wt.shape[0]
    a = wt.reshape(ci // 128, 128, n_out // 128, 128)
    return np.ascontiguousarray(a.transpose(2, 1, 0, 3).reshape(n_out, ci))


def _prep_shared(L0_attn_Wq, L0_attn_Wk, L0_attn_Wv, L0_attn_Wp,
                 L0_ffn_gate_w, L0_ffn_up_w, L0_ffn_down_w,
                 L0_ffn_gate_s, L0_ffn_up_s, L0_ffn_down_s):
    f = np.float32
    sh = {}
    sh["wq"] = _tile_stationary(np.ascontiguousarray(L0_attn_Wq.T, dtype=f), NQ * 128)
    sh["wk"] = _tile_stationary(np.ascontiguousarray(L0_attn_Wk.T, dtype=f), NKV * 128)
    sh["wv"] = _tile_stationary(np.ascontiguousarray(L0_attn_Wv.T, dtype=f), NKV * 128)
    sh["wp"] = np.ascontiguousarray(L0_attn_Wp.T, dtype=f)
    sh["wg"] = _tile_stationary(np.ascontiguousarray(L0_ffn_gate_w.T, dtype=f), NIC * 128)
    sh["wu"] = _tile_stationary(np.ascontiguousarray(L0_ffn_up_w.T, dtype=f), NIC * 128)
    sh["wd"] = np.ascontiguousarray(L0_ffn_down_w.T, dtype=f)
    sh["gs"] = np.ascontiguousarray(np.asarray(L0_ffn_gate_s, dtype=f).reshape(NIC, 128).T)
    sh["us"] = np.ascontiguousarray(np.asarray(L0_ffn_up_s, dtype=f).reshape(NIC, 128).T)
    sh["dsb"] = np.ascontiguousarray(
        np.broadcast_to(np.asarray(L0_ffn_down_s, dtype=f), (128, C)))
    inv = (1.0 / ROPE_BASE ** (np.arange(64, dtype=np.float64) / 64.0))
    posk = np.arange(T, dtype=np.float64)
    angk = np.outer(inv, posk)
    sh["tabkc"] = np.concatenate([np.cos(angk), np.cos(angk)], 0).astype(f)
    sh["tabks"] = np.concatenate([-np.sin(angk), np.sin(angk)], 0).astype(f)
    rot = np.zeros((128, 128), dtype=f)
    rot[(np.arange(128) + 64) % 128, np.arange(128)] = 1.0
    sh["rotm"] = rot
    sh["idm"] = np.eye(128, dtype=f)
    sh["onesr"] = np.ones((1, 128), dtype=f)
    return sh


def _prep_core(x, r, sh):
    f = np.float32
    d = dict(sh)
    d["xall"] = np.ascontiguousarray(x, dtype=f)
    d["xown"] = np.ascontiguousarray(x[r::4, :], dtype=f)
    posq = (4.0 * np.arange(TOWN, dtype=np.float64) + r)
    inv = (1.0 / ROPE_BASE ** (np.arange(64, dtype=np.float64) / 64.0))
    angq = np.outer(inv, posq)
    d["tabqc"] = np.concatenate([np.cos(angq), np.cos(angq)], 0).astype(f)
    d["tabqs"] = np.concatenate([-np.sin(angq), np.sin(angq)], 0).astype(f)
    m = np.zeros((2, 128, TOWN), dtype=f)
    for h2 in range(2):
        qoff = 4 * (128 * h2 + np.arange(128)) + r
        kk = np.arange(TOWN)
        m[h2] = np.where(kk[None, :] <= qoff[:, None], 0.0, NEG)
    d["maskadd"] = m.reshape(256, TOWN)
    return d


_NC_CACHE = {}


def kernel(**inputs):
    x = np.asarray(inputs["L0_attn_x"], dtype=np.float32)
    sh = _prep_shared(
        np.asarray(inputs["L0_attn_Wq"]), np.asarray(inputs["L0_attn_Wk"]),
        np.asarray(inputs["L0_attn_Wv"]), np.asarray(inputs["L0_attn_Wp"]),
        np.asarray(inputs["L0_ffn_gate_w"]), np.asarray(inputs["L0_ffn_up_w"]),
        np.asarray(inputs["L0_ffn_down_w"]), np.asarray(inputs["L0_ffn_gate_s"]),
        np.asarray(inputs["L0_ffn_up_s"]), np.asarray(inputs["L0_ffn_down_s"]))
    in_maps = []
    for c in range(8):
        b, r = c // 4, c % 4
        in_maps.append(_prep_core(x[b], r, sh))
    if "nc" not in _NC_CACHE:
        _NC_CACHE["nc"] = build_program()
    nc = _NC_CACHE["nc"]
    res = run_bass_kernel_spmd(nc, in_maps, core_ids=list(range(8)))
    full = np.empty((B, T, C), dtype=np.float32)
    for c in range(8):
        b, r = c // 4, c % 4
        full[b, r::4, :] = res.results[c]["out"]
    return full
